# version 11
# baseline (speedup 1.0000x reference)
import contextlib
import numpy as np
import ml_dtypes
import concourse.bass as bass
import concourse.mybir as mybir
import concourse.bass_utils as _bass_utils
from concourse.bass_utils import run_bass_kernel_spmd

if not getattr(_bass_utils, "_ldw_opt_patched", False):
    _orig_run_command = _bass_utils.run_command

    def _run_command_ldw(cmd, *a, **kw):
        if isinstance(cmd, list):
            cmd = ["--enable-ldw-opt=true" if c == "--enable-ldw-opt=false" else c
                   for c in cmd]
        return _orig_run_command(cmd, *a, **kw)

    _bass_utils.run_command = _run_command_ldw
    _bass_utils._ldw_opt_patched = True

F32 = mybir.dt.float32
BF16 = mybir.dt.bfloat16
I32 = mybir.dt.int32
AF = mybir.ActivationFunctionType
ALU = mybir.AluOpType
AX = mybir.AxisListType

PI = float(np.pi)

SEQ = 2
T = 4096
D = 1024
DH = 2048
NB = 8
E = 8
FF = 1408
L1 = 256
NT1 = L1 // 128
NC1 = SEQ * T // L1
L2 = 512
NT2 = L2 // 128
NC2 = SEQ * T // L2

C_ONE, C_NEG1, C_HALF, C_NEGHALF, C_PIH, C_EPS, C_NEG2PI, C_I2PI = range(8)

ENGS = ("sync", "gpsimd", "tensor", "vector", "scalar")


class Ctr:
    def __init__(self, nc, name):
        self.sem = nc.alloc_semaphore(name)
        self.n = 0


class Sched:
    def __init__(self):
        self.q = {e: [] for e in ENGS}

    def on(self, eng, fn):
        self.q[eng].append(fn)

    def wait(self, eng, ctr, val=None):
        v = ctr.n if val is None else val
        if v <= 0:
            return
        sem = ctr.sem
        self.on(eng, lambda e, s=sem, vv=v: e.wait_ge(s, vv))

    def op(self, eng, fn, inc=None, k=1):
        if inc is not None:
            inc.n += k
            sem = inc.sem
            self.on(eng, lambda e, f=fn, s=sem, kk=k: f(e).then_inc(s, kk))
        else:
            self.on(eng, fn)

    def dma(self, eng, out, in_, inc):
        inc.n += 16
        sem = inc.sem
        self.on(eng, lambda e, o=out, i=in_, s=sem: e.dma_start(out=o, in_=i).then_inc(s, 16))

    def dmat(self, eng, out, in_, inc):
        inc.n += 16
        sem = inc.sem
        self.on(eng, lambda e, o=out, i=in_, s=sem: e.dma_start_transpose(o, i).then_inc(s, 16))

    def emit(self, nc, name):
        with nc.Block(name) as block:
            @block.sync
            def _(e):
                for fn in self.q["sync"]:
                    fn(e)

            @block.gpsimd
            def _(e):
                for fn in self.q["gpsimd"]:
                    fn(e)

            @block.tensor
            def _(e):
                for fn in self.q["tensor"]:
                    fn(e)

            @block.vector
            def _(e):
                for fn in self.q["vector"]:
                    fn(e)

            @block.scalar
            def _(e):
                for fn in self.q["scalar"]:
                    fn(e)


def build_nc(debug=False, no_conv=False):
    nc = bass.Bass("TRN2", target_bir_lowering=False)

    x_h = nc.dram_tensor("x", [SEQ, T, D], F32, kind="ExternalInput").ap()
    w_in_h = nc.dram_tensor("w_in", [D, DH], F32, kind="ExternalInput").ap()
    b_in_h = nc.dram_tensor("b_in", [128, 16], F32, kind="ExternalInput").ap()
    w_gx_h = nc.dram_tensor("w_gx", [NB, 256, 128], F32, kind="ExternalInput").ap()
    b_gx_h = nc.dram_tensor("b_gx", [128, 8], F32, kind="ExternalInput").ap()
    w_ga_h = nc.dram_tensor("w_ga", [NB, 256, 128], F32, kind="ExternalInput").ap()
    b_ga_h = nc.dram_tensor("b_ga", [128, 8], F32, kind="ExternalInput").ap()
    arp_h = nc.dram_tensor("arp", [128, 8], F32, kind="ExternalInput").ap()
    aip_h = nc.dram_tensor("aip", [128, 8], F32, kind="ExternalInput").ap()
    w_out_h = nc.dram_tensor("w_out", [DH, D], F32, kind="ExternalInput").ap()
    bout_pl_h = nc.dram_tensor("bout_pl", [128, D], F32, kind="ExternalInput").ap()
    rw_h = nc.dram_tensor("rw", [D, E], F32, kind="ExternalInput").ap()
    w1_h = nc.dram_tensor("w1", [E, D, 2 * FF], F32, kind="ExternalInput").ap()
    w2_h = nc.dram_tensor("w2", [E, FF, D], F32, kind="ExternalInput").ap()
    cc_h = nc.dram_tensor("cc", [128, 8], F32, kind="ExternalInput").ap()

    y_h = nc.dram_tensor("y", [SEQ, T, D], F32, kind="ExternalOutput").ap()

    _scr = "ExternalOutput" if debug else "Internal"
    w1bf_h = nc.dram_tensor("w1bf", [E, 8, 128, 2 * FF], BF16, kind=_scr).ap()
    w2bf_h = nc.dram_tensor("w2bf", [E, 11, 128, D], BF16, kind=_scr).ap()
    xnscr_h = nc.dram_tensor("xnscr", [NC1, 128, 8, L1], BF16, kind=_scr).ap()
    pscr_h = nc.dram_tensor("pscr", [NC1, 128, NT1, D], F32, kind=_scr).ap()
    if debug:
        dbg_lnc = nc.dram_tensor("dbg_lnc", [128, NT1, 8], F32, kind="ExternalOutput").ap()
        dbg_xntm = nc.dram_tensor("dbg_xntm", [128, NT1, D], BF16, kind="ExternalOutput").ap()
        dbg_xtm = nc.dram_tensor("dbg_xtm", [128, NT1, D], F32, kind="ExternalOutput").ap()
        dbg_sq = nc.dram_tensor("dbg_sq", [128, 2048], F32, kind="ExternalOutput").ap()
        dbg_xh = nc.dram_tensor("dbg_xh", [128, 16, L1], F32, kind="ExternalOutput").ap()
        dbg_ga = nc.dram_tensor("dbg_ga", [128, 8, L1], F32, kind="ExternalOutput").ap()
        dbg_gx = nc.dram_tensor("dbg_gx", [128, 8, L1], F32, kind="ExternalOutput").ap()
        dbg_h = nc.dram_tensor("dbg_h", [128, 16, L1], BF16, kind="ExternalOutput").ap()
        dbg_msp = nc.dram_tensor("dbg_msp", [128, 8], F32, kind="ExternalOutput").ap()
        dbg_thc = nc.dram_tensor("dbg_thc", [128, 8, L1], F32, kind="ExternalOutput").ap()
        dbg_phi = nc.dram_tensor("dbg_phi", [128, 8, L1], F32, kind="ExternalOutput").ap()
        dbg_rho = nc.dram_tensor("dbg_rho", [128, 8, L1], F32, kind="ExternalOutput").ap()
        dbg_nrm = nc.dram_tensor("dbg_nrm", [128, 8, L1], F32, kind="ExternalOutput").ap()
        dbg_sp = nc.dram_tensor("dbg_sp", [128, 8, L1], F32, kind="ExternalOutput").ap()
        dbg_cp = nc.dram_tensor("dbg_cp", [128, 8, L1], F32, kind="ExternalOutput").ap()
        dbg_s2 = nc.dram_tensor("dbg_s2", [NT1, 128, 1], F32, kind="ExternalOutput").ap()

    cc = nc.alloc_sbuf_tensor("cc_sb", [128, 8], F32)
    bout_pl = nc.alloc_sbuf_tensor("bout_sb", [128, D], F32)
    rw_sb = nc.alloc_sbuf_tensor("rw_sb", [128, 8, E], BF16)
    bin_c = nc.alloc_sbuf_tensor("bin_c", [128, 16], F32)
    bgx_c = nc.alloc_sbuf_tensor("bgx_c", [128, 8], F32)
    bga_c = nc.alloc_sbuf_tensor("bga_c", [128, 8], F32)
    arp_c = nc.alloc_sbuf_tensor("arp_c", [128, 8], F32)
    aip_c = nc.alloc_sbuf_tensor("aip_c", [128, 8], F32)
    msp_c = nc.alloc_sbuf_tensor("msp_c", [128, 8], F32)
    tc0 = nc.alloc_sbuf_tensor("tc0", [128, 8], F32)
    tc1 = nc.alloc_sbuf_tensor("tc1", [128, 8], F32)
    tc2 = nc.alloc_sbuf_tensor("tc2", [128, 8], F32)
    tc3 = nc.alloc_sbuf_tensor("tc3", [128, 8], F32)
    bar = nc.alloc_sbuf_tensor("bar", [128, 8], F32)
    ones_pl = nc.alloc_sbuf_tensor("ones_pl", [128, L1], F32)

    s_ld = Ctr(nc, "s_ld")
    s_gw = Ctr(nc, "s_gw")
    s_pe = Ctr(nc, "s_pe")
    s_dv = Ctr(nc, "s_dv")
    s_ac = Ctr(nc, "s_ac")
    s_st = Ctr(nc, "s_st")
    s_ph = Ctr(nc, "s_ph")
    s_p2l = Ctr(nc, "s_p2l")
    s_gp = Ctr(nc, "s_gp")

    def ccol(i):
        return cc[:, i:i + 1]

    def gflush(s, inc):
        s.op("gpsimd", lambda g: g.tensor_copy(bar[:, 6:7], cc[:, 0:1]), inc=inc)

    def vflush(s, inc):
        s.op("vector", lambda v: v.tensor_copy(bar[:, 2:3], cc[:, 0:1]), inc=inc)

    def aflush(s, inc):
        s.op("scalar", lambda sc: sc.copy(bar[:, 3:4], cc[:, 0:1]), inc=inc)

    s = Sched()
    stack = contextlib.ExitStack()
    w_in_sb = stack.enter_context(nc.sbuf_tensor("w_in_sb", [128, 8, DH], BF16))
    wgx_sb = stack.enter_context(nc.sbuf_tensor("wgx_sb", [128, NB, 2, 128], BF16))
    wga_sb = stack.enter_context(nc.sbuf_tensor("wga_sb", [128, NB, 2, 128], BF16))
    w_out_sb = stack.enter_context(nc.sbuf_tensor("w_out_sb", [128, 16, D], BF16))
    x_tm2 = [stack.enter_context(nc.sbuf_tensor(f"x_tm{i}", [128, NT1, D], F32))
             for i in range(2)]
    xn_tm = stack.enter_context(nc.sbuf_tensor("xn_tm", [128, NT1, D], BF16))
    xn_fm = stack.enter_context(nc.sbuf_tensor("xn_fm", [128, 8, L1], BF16))
    xh_f = stack.enter_context(nc.sbuf_tensor("xh_f", [128, 16, L1], F32))
    xh_b = stack.enter_context(nc.sbuf_tensor("xh_b", [128, 16, L1], BF16))
    GA = stack.enter_context(nc.sbuf_tensor("GA", [128, 8, L1], F32))
    GX = stack.enter_context(nc.sbuf_tensor("GX", [128, 8, L1], F32))
    T0 = stack.enter_context(nc.sbuf_tensor("T0", [128, 8, L1], F32))
    T1 = stack.enter_context(nc.sbuf_tensor("T1", [128, 8, L1], F32))
    T2 = stack.enter_context(nc.sbuf_tensor("T2", [128, 8, L1], F32))
    T3 = stack.enter_context(nc.sbuf_tensor("T3", [128, 8, L1], F32))
    T4 = stack.enter_context(nc.sbuf_tensor("T4", [128, 8, L1], F32))
    T5 = stack.enter_context(nc.sbuf_tensor("T5", [128, 8, L1], F32))
    h_bf = stack.enter_context(nc.sbuf_tensor("h_bf", [128, 16, L1], BF16))
    partial = stack.enter_context(nc.sbuf_tensor("partial", [128, NT1, D], F32))
    carry_r = stack.enter_context(nc.sbuf_tensor("carry_r", [128, 8], F32))
    carry_i = stack.enter_context(nc.sbuf_tensor("carry_i", [128, 8], F32))
    lnc = stack.enter_context(nc.sbuf_tensor("lnc", [128, NT1, 8], F32))
    s2c = [stack.enter_context(nc.sbuf_tensor(f"s2c{i}", [128, 1], F32))
           for i in range(NT1)]
    p_m0 = stack.enter_context(nc.psum_tensor("p_m0", [128, 512], F32))
    p_m1 = stack.enter_context(nc.psum_tensor("p_m1", [128, 512], F32))
    p_w0 = stack.enter_context(nc.psum_tensor("p_w0", [128, D], F32))
    p_w1 = stack.enter_context(nc.psum_tensor("p_w1", [128, D], F32))
    p_m = [p_m0, p_m1]
    p_w = [p_w0, p_w1]

    KIap = T5.ap().bitcast(I32)

    s.dma("sync", cc[:], cc_h, s_ld)
    s.dma("sync", bout_pl[:], bout_pl_h, s_ld)
    s.dma("sync", bin_c[:], b_in_h, s_ld)
    s.dma("sync", bgx_c[:], b_gx_h, s_ld)
    s.dma("sync", bga_c[:], b_ga_h, s_ld)
    s.dma("sync", arp_c[:], arp_h, s_ld)
    s.dma("sync", aip_c[:], aip_h, s_ld)
    LD_CONSTS = s_ld.n

    s.dma("gpsimd", w_in_sb[:], w_in_h.rearrange("(k p) h -> p k h", p=128), s_gw)
    s.dma("gpsimd", wgx_sb[:], w_gx_h.rearrange("b (kt p) o -> p b kt o", p=128), s_gw)
    s.dma("gpsimd", wga_sb[:], w_ga_h.rearrange("b (kt p) o -> p b kt o", p=128), s_gw)
    s.dma("gpsimd", w_out_sb[:], w_out_h.rearrange("(k p) h -> p k h", p=128), s_gw)
    s.dma("gpsimd", rw_sb[:], rw_h.rearrange("(k p) e -> p k e", p=128), s_gw)
    GW_ATTN = s_gw.n
    if not no_conv:
        for e in range(E):
            s.dma("gpsimd", w1bf_h[e], w1_h[e].rearrange("(kt p) f -> kt p f", p=128), s_gw)
            s.dma("gpsimd", w2bf_h[e], w2_h[e].rearrange("(ft p) h -> ft p h", p=128), s_gw)
    GW_CONV = s_gw.n

    s.wait("vector", s_ld, LD_CONSTS)
    s.op("vector", lambda v: v.memset(ones_pl[:], 1.0))
    s.op("vector", lambda v: v.memset(carry_r[:], 0.0))
    s.op("vector", lambda v: v.memset(carry_i[:], 0.0), inc=s_dv)
    V_INIT = s_dv.n
    s.wait("scalar", s_ld, LD_CONSTS)
    s.op("scalar", lambda sc: sc.activation(tc0[:], arp_c[:], AF.Abs))
    s.op("scalar", lambda sc: sc.activation(tc3[:], arp_c[:], AF.Relu))
    s.op("scalar", lambda sc: sc.activation(tc1[:], tc0[:], AF.Exp, scale=ccol(C_NEG1)))
    s.op("scalar", lambda sc: sc.copy(bar[:, 4:5], cc[:, 0:1]))
    s.op("scalar", lambda sc: sc.activation(tc0[:], tc1[:], AF.Ln, bias=ccol(C_ONE)))
    aflush(s, s_ac)
    A_SP = s_ac.n
    s.wait("vector", s_ac, A_SP)
    s.op("vector", lambda v: v.tensor_add(msp_c[:], tc0[:], tc3[:]))
    vflush(s, s_dv)
    s.op("vector", lambda v: v.tensor_single_scalar(msp_c[:], msp_c[:], -8.0, ALU.mult))
    vflush(s, s_dv)
    V_MSP = s_dv.n

    pm_free = [("", 0), ("", 0)]
    pw_free = [0, 0]
    V_end = 0
    V_scans_prev = 0
    G_carry = 0
    G_rot = 0
    Vend_hist = [0] * (NC1 + 2)
    L_t_prev = 0
    M_xn_prev = 0
    P_bdl_prev = 0
    P_wo_prev = 0
    S_p_prev = 0
    mm_rot = 0
    wo_rot = 0

    x_r = x_h.rearrange("s (c tt p) ch -> (s c) p tt ch", tt=NT1, p=128)

    xr = xh_f.ap().rearrange("p (a ct) t -> p a ct t", a=2)[:, 0]
    xi = xh_f.ap().rearrange("p (a ct) t -> p a ct t", a=2)[:, 1]
    hr = h_bf.ap().rearrange("p (a ct) t -> p a ct t", a=2)[:, 0]
    hi = h_bf.ap().rearrange("p (a ct) t -> p a ct t", a=2)[:, 1]
    mspb = msp_c.ap().broadcast_to([128, 8, L1])
    aipb = aip_c.ap().broadcast_to([128, 8, L1])
    sqflat = T0.ap().rearrange("p a b -> p (a b)")
    sqv = [sqflat[:, tt * D:(tt + 1) * D] for tt in range(NT1)]

    for c in range(NC1):
        ci = c % (NC1 // SEQ)
        x_tm = x_tm2[c % 2]
        if c >= 2:
            s.wait("sync", s_dv, Vend_hist[c - 2])
        s.dma("sync", x_tm[:], x_r[c], s_ld)
        L_x = s_ld.n

        if ci == 0 and c > 0:
            s.op("gpsimd", lambda g: g.memset(carry_r[:], 0.0))
            s.op("gpsimd", lambda g: g.memset(carry_i[:], 0.0))
            gflush(s, s_gp)

        s.wait("vector", s_ld, L_x)
        s.wait("vector", s_ld, L_t_prev)
        for tt in range(NT1):
            s.op("vector", lambda v, tt=tt, xtm=x_tm: v.reduce_sum(
                lnc[:, tt, 0:1], xtm[:, tt, :], AX.X), inc=s_dv)
        s.wait("scalar", s_ld, L_x)
        s.wait("scalar", s_gp, G_carry)
        for tt in range(NT1):
            s.op("scalar", lambda sc, tt=tt, xtm=x_tm: sc.activation(
                sqv[tt], xtm[:, tt, :], AF.Square), inc=s_ac)
            A_sq = s_ac.n
            s.wait("vector", s_ac, A_sq)
            s.op("vector", lambda v, tt=tt: v.reduce_sum(s2c[tt][:], sqv[tt], AX.X))
        for tt in range(NT1):
            s.op("vector", lambda v, tt=tt: v.tensor_single_scalar(
                lnc[:, tt, 2:3], lnc[:, tt, 0:1], 1.0 / D, ALU.mult))
        for tt in range(NT1):
            s.op("vector", lambda v, tt=tt: v.tensor_single_scalar(
                lnc[:, tt, 5:6], s2c[tt][:], 1.0 / D, ALU.mult))
        for tt in range(NT1):
            s.op("vector", lambda v, tt=tt: v.tensor_mul(
                lnc[:, tt, 3:4], lnc[:, tt, 2:3], lnc[:, tt, 2:3]))
        for tt in range(NT1):
            s.op("vector", lambda v, tt=tt: v.tensor_sub(
                lnc[:, tt, 1:2], lnc[:, tt, 5:6], lnc[:, tt, 3:4]))
        vflush(s, s_dv)
        V_var = s_dv.n
        s.wait("scalar", s_dv, V_var)
        for tt in range(NT1):
            s.op("scalar", lambda sc, tt=tt: sc.activation(
                lnc[:, tt, 4:5], lnc[:, tt, 1:2], AF.Ln, bias=ccol(C_EPS)))
        for tt in range(NT1):
            s.op("scalar", lambda sc, tt=tt: sc.activation(
                lnc[:, tt, 0:1], lnc[:, tt, 4:5], AF.Exp, scale=ccol(C_NEGHALF)))
        aflush(s, s_ac)
        A_rstd = s_ac.n
        s.wait("vector", s_ac, A_rstd)
        for tt in range(NT1):
            s.op("vector", lambda v, tt=tt, xtm=x_tm: v.tensor_scalar(
                xn_tm[:, tt, :], xtm[:, tt, :], lnc[:, tt, 2:3], lnc[:, tt, 0:1],
                ALU.subtract, ALU.mult), inc=s_dv)
        V_ln = s_dv.n

        s.wait("sync", s_dv, V_ln)
        if debug and c == 0:
            s.dma("sync", dbg_lnc, lnc[:], s_ld)
            s.dma("sync", dbg_xntm, xn_tm[:], s_ld)
            s.dma("sync", dbg_xtm, x_tm[:], s_ld)
            s.dma("sync", dbg_sq, sqflat, s_ld)
            for tt in range(NT1):
                s.dma("sync", dbg_s2[tt], s2c[tt][:], s_ld)
            s.wait("vector", s_ld)
            s.wait("scalar", s_ld)
        s.wait("sync", s_pe, M_xn_prev)
        for tt in range(NT1):
            s.dmat("sync", xn_fm[:, :, tt * 128:(tt + 1) * 128], xn_tm[:, tt, :], s_ld)
        L_t = s_ld.n
        s.wait("sync", s_ld, L_t)
        s.dma("sync", xnscr_h[c], xn_fm[:], s_ld)
        L_t_prev = s_ld.n

        if c == 0:
            s.wait("tensor", s_gw, GW_ATTN)
        s.wait("tensor", s_ld, L_t)
        for ht in range(16):
            bk = mm_rot % 2
            mm_rot += 1
            nm, v = pm_free[bk]
            if nm == "ac":
                s.wait("tensor", s_ac, v)
            for kt in range(8):
                s.op("tensor", lambda t, ht=ht, kt=kt, bk=bk: t.matmul(
                    p_m[bk][:, 0:L1], w_in_sb[:, kt, ht * 128:(ht + 1) * 128],
                    xn_fm[:, kt, :], start=(kt == 0), stop=(kt == 7)),
                    inc=(s_pe if kt == 7 else None))
            P_ht = s_pe.n
            s.wait("vector", s_pe, P_ht)
            if ht == 0:
                s.wait("vector", s_gp, G_rot)
            s.op("vector", lambda v, ht=ht, bk=bk: v.tensor_scalar(
                xh_f[:, ht, :], p_m[bk][:, 0:L1], bin_c[:, ht:ht + 1], None,
                ALU.add), inc=s_dv)
            V_xh = s_dv.n
            if ht == 0:
                s.wait("scalar", s_pe, P_bdl_prev)
                s.wait("scalar", s_gp, G_carry)
            s.wait("scalar", s_dv, V_xh)
            s.op("scalar", lambda sc, ht=ht, bk=bk: sc.activation(
                xh_b[:, ht, :], p_m[bk][:, 0:L1], AF.Identity,
                bias=bin_c[:, ht:ht + 1]), inc=s_ac)
            pm_free[bk] = ("ac", s_ac.n)

        s.wait("scalar", s_dv, V_scans_prev)
        s.wait("scalar", s_gp, G_carry)
        for wsb, bcol, gpl in ((wgx_sb, bgx_c, GX), (wga_sb, bga_c, GA)):
            for b in range(NB):
                bk = mm_rot % 2
                mm_rot += 1
                nm, v = pm_free[bk]
                if nm == "ac":
                    s.wait("tensor", s_ac, v)
                for kt in range(2):
                    s.op("tensor", lambda t, wsb=wsb, b=b, kt=kt, bk=bk: t.matmul(
                        p_m[bk][:, 0:L1], wsb[:, b, kt, :],
                        xh_b[:, b * 2 + kt, :], start=(kt == 0), stop=(kt == 1)),
                        inc=(s_pe if kt == 1 else None))
                P_g = s_pe.n
                s.wait("scalar", s_pe, P_g)
                s.op("scalar", lambda sc, gpl=gpl, b=b, bk=bk, bcol=bcol: sc.activation(
                    gpl[:, b, :], p_m[bk][:, 0:L1], AF.Sigmoid,
                    bias=bcol[:, b:b + 1]), inc=s_ac)
                pm_free[bk] = ("ac", s_ac.n)
        P_bdl_prev = s_pe.n
        A_gates = s_ac.n
        M_xn_prev = s_pe.n

        if debug and c == 0:
            s.wait("sync", s_ac, A_gates)
            s.dma("sync", dbg_xh, xh_f[:], s_ld)
            s.dma("sync", dbg_ga, GA[:], s_ld)
            s.dma("sync", dbg_gx, GX[:], s_ld)
            s.dma("sync", dbg_msp, msp_c[:], s_ld)
            s.wait("vector", s_ld)
            s.wait("scalar", s_ld)
            s.wait("tensor", s_ld)
        if c == 0:
            s.wait("vector", s_dv, max(V_MSP, V_INIT))
        s.wait("vector", s_ac, A_gates)
        s.op("vector", lambda v: v.tensor_tensor(T0[:], GA[:], mspb, ALU.mult))
        s.op("vector", lambda v: v.tensor_tensor(T1[:], GA[:], aipb, ALU.mult),
             inc=s_dv)
        V_l = s_dv.n
        s.wait("vector", s_gp, G_carry)
        for ct in range(8):
            s.op("vector", lambda v, ct=ct: v.tensor_tensor_scan(
                T2[:, ct, :], ones_pl[:], T1[:, ct, :], 0.0, ALU.mult, ALU.add))
        s.op("vector", lambda v: v.tensor_single_scalar(
            T1[:], T2[:], 1.0 / (2 * PI), ALU.mult))
        s.op("vector", lambda v: v.tensor_copy(KIap[:], T1[:]))
        s.op("vector", lambda v: v.tensor_copy(T1[:], KIap[:]))
        s.op("vector", lambda v: v.scalar_tensor_tensor(
            T1[:], T1[:], -2 * PI, T2[:], ALU.mult, ALU.add), inc=s_dv)
        V_phi = s_dv.n
        if debug and c == 0:
            s.wait("sync", s_dv, V_phi)
            s.dma("sync", dbg_thc, T2[:], s_ld)
            s.dma("sync", dbg_phi, T1[:], s_ld)
            s.wait("vector", s_ld)
            s.wait("scalar", s_ld)
        s.wait("scalar", s_dv, V_phi)
        s.op("scalar", lambda sc: sc.activation(T2[:], T0[:], AF.Exp))
        s.op("scalar", lambda sc: sc.activation(T0[:], T2[:], AF.Square), inc=s_ac)
        A_q = s_ac.n
        s.wait("vector", s_ac, A_q)
        s.op("vector", lambda v: v.tensor_scalar(
            T0[:], T0[:], -1.0, 1.0, ALU.mult, ALU.add))
        s.op("vector", lambda v: v.tensor_single_scalar(
            T0[:], T0[:], 1e-6, ALU.max), inc=s_dv)
        V_r2 = s_dv.n
        s.wait("scalar", s_dv, V_r2)
        s.op("scalar", lambda sc: sc.activation(T3[:], T0[:], AF.Ln))
        s.op("scalar", lambda sc: sc.activation(T0[:], T3[:], AF.Exp,
                                                scale=ccol(C_HALF)), inc=s_ac)
        s.wait("scalar", s_dv, V_phi)
        s.op("scalar", lambda sc: sc.activation(T3[:], T1[:], AF.Sin))
        s.op("scalar", lambda sc: sc.activation(T5[:], T1[:], AF.Abs))
        s.op("scalar", lambda sc: sc.activation(
            T4[:], T5[:], AF.Sin, scale=ccol(C_NEG1), bias=ccol(C_PIH)), inc=s_ac)
        A_trig = s_ac.n
        if debug and c == 0:
            s.wait("sync", s_ac, A_trig)
            s.dma("sync", dbg_rho, T2[:], s_ld)
            s.dma("sync", dbg_nrm, T0[:], s_ld)
            s.dma("sync", dbg_sp, T3[:], s_ld)
            s.dma("sync", dbg_cp, T4[:], s_ld)
            s.wait("vector", s_ld)
            s.wait("scalar", s_ld)
        s.wait("gpsimd", s_ac, A_trig)
        s.wait("gpsimd", s_dv, V_phi)
        s.op("gpsimd", lambda g: g.tensor_mul(T1[:], GX[:], T0[:]))
        s.op("gpsimd", lambda g: g.tensor_mul(GX[:], T4[:], T1[:]))
        s.op("gpsimd", lambda g: g.tensor_mul(T0[:], T3[:], T1[:]))
        s.op("gpsimd", lambda g: g.tensor_mul(T1[:], GX[:], xr))
        s.op("gpsimd", lambda g: g.tensor_mul(T5[:], T0[:], xi))
        s.op("gpsimd", lambda g: g.tensor_add(T1[:], T1[:], T5[:]))
        s.op("gpsimd", lambda g: g.tensor_mul(GA[:], GX[:], xi))
        s.op("gpsimd", lambda g: g.tensor_mul(T5[:], T0[:], xr))
        s.op("gpsimd", lambda g: g.tensor_sub(GA[:], GA[:], T5[:]))
        gflush(s, s_gp)
        G_rot = s_gp.n
        s.wait("vector", s_gp, G_rot)
        for ct in range(8):
            s.op("vector", lambda v, ct=ct: v.tensor_tensor_scan(
                GX[:, ct, :], T2[:, ct, :], T1[:, ct, :], carry_r[:, ct:ct + 1],
                ALU.mult, ALU.add))
        for ct in range(8):
            s.op("vector", lambda v, ct=ct: v.tensor_tensor_scan(
                T0[:, ct, :], T2[:, ct, :], GA[:, ct, :], carry_i[:, ct:ct + 1],
                ALU.mult, ALU.add))
        vflush(s, s_dv)
        V_scans = s_dv.n
        V_scans_prev = V_scans
        s.wait("gpsimd", s_dv, V_scans)
        s.wait("gpsimd", s_pe, P_wo_prev)
        s.op("gpsimd", lambda g: g.tensor_mul(T1[:], T4[:], GX[:]))
        s.op("gpsimd", lambda g: g.tensor_mul(T2[:], T3[:], T0[:]))
        s.op("gpsimd", lambda g: g.tensor_sub(hr, T1[:], T2[:]))
        s.op("gpsimd", lambda g: g.tensor_mul(T1[:], T3[:], GX[:]))
        s.op("gpsimd", lambda g: g.tensor_mul(T2[:], T4[:], T0[:]))
        s.op("gpsimd", lambda g: g.tensor_add(hi, T1[:], T2[:]))
        LC = L1 - 1
        s.op("gpsimd", lambda g: g.tensor_mul(tc0[:], T4[:, :, LC], GX[:, :, LC]))
        s.op("gpsimd", lambda g: g.tensor_mul(tc1[:], T3[:, :, LC], T0[:, :, LC]))
        s.op("gpsimd", lambda g: g.tensor_mul(tc2[:], T3[:, :, LC], GX[:, :, LC]))
        s.op("gpsimd", lambda g: g.tensor_mul(tc3[:], T4[:, :, LC], T0[:, :, LC]))
        s.op("gpsimd", lambda g: g.tensor_sub(carry_r[:], tc0[:], tc1[:]))
        s.op("gpsimd", lambda g: g.tensor_add(carry_i[:], tc2[:], tc3[:]))
        gflush(s, s_gp)
        G_carry = s_gp.n
        V_h = s_gp.n

        if debug and c == 0:
            s.wait("sync", s_gp, V_h)
            s.dma("sync", dbg_h, h_bf[:], s_ld)
            s.wait("vector", s_ld)
            s.wait("scalar", s_ld)
            s.wait("tensor", s_ld)
        s.wait("tensor", s_gp, V_h)
        for tt in range(NT1):
            bk = wo_rot % 2
            wo_rot += 1
            if pw_free[bk]:
                s.wait("tensor", s_dv, pw_free[bk])
            for kt in range(16):
                for nh in range(2):
                    s.op("tensor", lambda t, tt=tt, nh=nh, kt=kt, bk=bk: t.matmul(
                        p_w[bk][:, nh * 512:(nh + 1) * 512],
                        h_bf[:, kt, tt * 128:(tt + 1) * 128],
                        w_out_sb[:, kt, nh * 512:(nh + 1) * 512],
                        start=(kt == 0), stop=(kt == 15)),
                        inc=(s_pe if (nh == 1 and kt == 15) else None))
            P_wo = s_pe.n
            s.wait("vector", s_pe, P_wo)
            if tt == 0 and S_p_prev:
                s.wait("vector", s_st, S_p_prev)
            for nh in range(2):
                sl = slice(nh * 512, (nh + 1) * 512)
                s.op("vector", lambda v, tt=tt, bk=bk, sl=sl, xtm=x_tm: v.scalar_tensor_tensor(
                    partial[:, tt, sl], p_w[bk][:, sl], 1.0, xtm[:, tt, sl],
                    ALU.mult, ALU.add))
                s.op("vector", lambda v, tt=tt, sl=sl: v.tensor_add(
                    partial[:, tt, sl], partial[:, tt, sl], bout_pl[:, sl]),
                    inc=s_dv)
            pw_free[bk] = s_dv.n
        P_wo_prev = s_pe.n
        V_end = s_dv.n
        Vend_hist[c] = V_end

        s.wait("sync", s_dv, V_end)
        s.dma("sync", pscr_h[c], partial[:], s_st)
        S_p_prev = s_st.n

    s.wait("sync", s_st, s_st.n)
    s.wait("sync", s_ld, s_ld.n)
    s.op("sync", lambda e: e.nop(), inc=s_ph)
    s.op("gpsimd", lambda e: e.nop(), inc=s_ph)
    s.op("tensor", lambda e: e.nop(), inc=s_ph)
    s.op("vector", lambda v: v.memset(bar[:, 0:1], 1.0), inc=s_ph)
    s.op("scalar", lambda sc: sc.memzero(bar[:, 1:2]), inc=s_ph)
    PH_BAR = s_ph.n

    s.emit(nc, "ph1")
    stack.close()

    s = Sched()
    stack = contextlib.ExitStack()
    xn2 = [stack.enter_context(nc.sbuf_tensor(f"xn2_{i}", [128, 8, L2], BF16))
           for i in range(2)]
    part2 = stack.enter_context(nc.sbuf_tensor("part2", [128, NT2, D], F32))
    gbuf = stack.enter_context(nc.sbuf_tensor("gbuf", [128, 11, L2], BF16))
    silu_sb = [stack.enter_context(nc.sbuf_tensor(f"silu_{i}", [128, L2], F32))
               for i in range(2)]
    w1buf = [stack.enter_context(nc.sbuf_tensor(f"w1buf_{i}", [128, 8, 2 * FF], BF16))
             for i in range(2)]
    w2buf = [stack.enter_context(nc.sbuf_tensor(f"w2buf_{i}", [128, 11, D], BF16))
             for i in range(2)]
    lp = stack.enter_context(nc.sbuf_tensor("lp", [128, NT2, E], F32))
    pr4 = stack.enter_context(nc.sbuf_tensor("pr4", [128, NT2, E], F32))
    sm0 = stack.enter_context(nc.sbuf_tensor("sm0", [128, NT2], F32))
    sm1 = stack.enter_context(nc.sbuf_tensor("sm1", [128, NT2], F32))
    p_y00 = stack.enter_context(nc.psum_tensor("p_y00", [128, 512], F32))
    p_y01 = stack.enter_context(nc.psum_tensor("p_y01", [128, 512], F32))
    p_y10 = stack.enter_context(nc.psum_tensor("p_y10", [128, 512], F32))
    p_y11 = stack.enter_context(nc.psum_tensor("p_y11", [128, 512], F32))
    p_o0 = stack.enter_context(nc.psum_tensor("p_o0", [128, D], F32))
    p_o1 = stack.enter_context(nc.psum_tensor("p_o1", [128, D], F32))
    p_y = [[p_y00, p_y01], [p_y10, p_y11]]
    p_o = [p_o0, p_o1]

    for eng in ENGS:
        s.wait(eng, s_ph, PH_BAR)
    s.wait("gpsimd", s_gw, GW_CONV)

    py_free = [0, 0]
    po_free = [0, 0]
    xn2_freeby = [0, 0]
    wslot_freeby = [0, 0]
    S_part_prev = 0
    ft_rot = 0
    tt_rot = 0

    y_r = y_h.rearrange("s (k tt p) ch -> (s k) p tt ch", tt=NT2, p=128)
    prv = p_y00.ap().rearrange("p (tt e) -> p tt e", e=E)[:, 0:NT2, :]

    for c2 in range(NC2):
        buf = c2 % 2
        if xn2_freeby[buf]:
            s.wait("sync", s_pe, xn2_freeby[buf])
        for hh in range(2):
            s.dma("sync", xn2[buf][:, :, hh * L1:(hh + 1) * L1], xnscr_h[2 * c2 + hh],
                  s_ld)
        L_xn2 = s_ld.n
        if S_part_prev:
            s.wait("sync", s_st, S_part_prev)
        for hh in range(2):
            s.dma("sync", part2[:, 2 * hh:2 * hh + 2, :], pscr_h[2 * c2 + hh], s_p2l)
        P2L = s_p2l.n

        s.wait("tensor", s_ld, L_xn2)
        if py_free[0]:
            s.wait("tensor", s_dv, py_free[0])
        for tt in range(NT2):
            for kt in range(8):
                s.op("tensor", lambda t, tt=tt, kt=kt, buf=buf: t.matmul(
                    p_y00[:, tt * E:(tt + 1) * E],
                    xn2[buf][:, kt, tt * 128:(tt + 1) * 128],
                    rw_sb[:, kt, :], start=(kt == 0), stop=(kt == 7)),
                    inc=(s_pe if (tt == NT2 - 1 and kt == 7) else None))
        P_r = s_pe.n
        s.wait("vector", s_pe, P_r)
        s.op("vector", lambda v: v.tensor_reduce(
            sm0[:], prv, AX.X, ALU.max, negate=True))
        vflush(s, s_dv)
        V_nm = s_dv.n
        s.wait("scalar", s_dv, V_nm)
        for tt in range(NT2):
            s.op("scalar", lambda sc, tt=tt: sc.activation(
                pr4[:, tt, :], prv[:, tt, :], AF.Exp, bias=sm0[:, tt:tt + 1]))
        aflush(s, s_ac)
        A_exp = s_ac.n
        s.wait("vector", s_ac, A_exp)
        s.op("vector", lambda v: v.reduce_sum(sm1[:], pr4[:], AX.X), inc=s_dv)
        py_free[0] = s_dv.n
        vsp = lambda: s.op("vector", lambda v: v.tensor_copy(bar[:, 5:6], cc[:, 1:2]))
        vsp()
        s.op("vector", lambda v: v.reciprocal(sm0[:], sm1[:]))
        vsp()
        s.op("vector", lambda v: v.tensor_tensor(
            pr4[:], pr4[:], sm0.ap().broadcast_to([128, NT2, E]), ALU.mult))
        vsp()
        s.op("vector", lambda v: v.tensor_reduce(sm1[:], pr4[:], AX.X, ALU.max))
        vsp()
        s.op("vector", lambda v: v.tensor_tensor(
            lp[:], pr4[:], sm1.ap().broadcast_to([128, NT2, E]), ALU.is_lt))
        vsp()
        s.op("vector", lambda v: v.tensor_mul(lp[:], lp[:], pr4[:]))
        vsp()
        s.op("vector", lambda v: v.tensor_reduce(sm1[:], lp[:], AX.X, ALU.max))
        vsp()
        s.op("vector", lambda v: v.tensor_tensor(
            lp[:], pr4[:], sm1.ap().broadcast_to([128, NT2, E]), ALU.is_ge))
        vsp()
        s.op("vector", lambda v: v.tensor_mul(lp[:], lp[:], pr4[:]))
        vflush(s, s_dv)
        V_lp = s_dv.n

        for e in range(E):
            g = c2 * E + e
            slot = g % 2
            if wslot_freeby[slot]:
                s.wait("gpsimd", s_pe, wslot_freeby[slot])
            s.dma("gpsimd", w1buf[slot][:], w1bf_h[e].rearrange("kt p f -> p kt f"),
                  s_gw)
            s.dma("gpsimd", w2buf[slot][:], w2bf_h[e].rearrange("ft p h -> p ft h"),
                  s_gw)
            G_w = s_gw.n

            s.wait("tensor", s_gw, G_w)
            if e == 0:
                s.wait("tensor", s_dv, V_lp)
            for ft in range(11):
                par = ft_rot % 2
                ft_rot += 1
                if py_free[par]:
                    s.wait("tensor", s_dv, py_free[par])
                for half in range(2):
                    fcol = (ft if half == 0 else ft + 11) * 128
                    for kt in range(8):
                        s.op("tensor", lambda t, slot=slot, fcol=fcol, kt=kt,
                             par=par, half=half, buf=buf: t.matmul(
                            p_y[par][half][:],
                            w1buf[slot][:, kt, fcol:fcol + 128],
                            xn2[buf][:, kt, :],
                            start=(kt == 0), stop=(kt == 7)),
                            inc=(s_pe if (half == 1 and kt == 7) else None))
                P_yft = s_pe.n
                s.wait("scalar", s_pe, P_yft)
                s.op("scalar", lambda sc, par=par: sc.activation(
                    silu_sb[par][:], p_y[par][0][:], AF.Silu), inc=s_ac)
                A_sil = s_ac.n
                s.wait("vector", s_ac, A_sil)
                s.op("vector", lambda v, ft=ft, par=par: v.tensor_mul(
                    gbuf[:, ft, :], silu_sb[par][:], p_y[par][1][:]), inc=s_dv)
                py_free[par] = s_dv.n
            V_g = s_dv.n
            if e == E - 1:
                xn2_freeby[buf] = s_pe.n

            s.wait("tensor", s_dv, V_g)
            if e == 0:
                s.wait("vector", s_p2l, P2L)
            for tt in range(NT2):
                po = tt_rot % 2
                tt_rot += 1
                if po_free[po]:
                    s.wait("tensor", s_dv, po_free[po])
                for ft in range(11):
                    for half in range(2):
                        s.op("tensor", lambda t, tt=tt, ft=ft, half=half, po=po,
                             slot=slot: t.matmul(
                            p_o[po][:, half * 512:(half + 1) * 512],
                            gbuf[:, ft, tt * 128:(tt + 1) * 128],
                            w2buf[slot][:, ft, half * 512:(half + 1) * 512],
                            start=(ft == 0), stop=(ft == 10)),
                            inc=(s_pe if (ft == 10 and half == 1) else None))
                P_oms = s_pe.n
                s.wait("vector", s_pe, P_oms)
                s.op("vector", lambda v, tt=tt, po=po, e=e: v.scalar_tensor_tensor(
                    part2[:, tt, :], p_o[po][:], lp[:, tt, e:e + 1],
                    part2[:, tt, :], ALU.mult, ALU.add), inc=s_dv)
                po_free[po] = s_dv.n
            wslot_freeby[slot] = s_pe.n
        V_acc_end = s_dv.n

        s.wait("sync", s_dv, V_acc_end)
        s.dma("sync", y_r[c2], part2[:], s_st)
        S_part_prev = s_st.n

    s.wait("sync", s_st, s_st.n)

    s.emit(nc, "ph2")
    stack.close()
    return nc


def make_in_maps(L0_attn_x, L0_attn_w_in, L0_attn_b_in, L0_attn_w_gx, L0_attn_b_gx,
                 L0_attn_w_ga, L0_attn_b_ga, L0_attn_a_real_param, L0_attn_a_imag_param,
                 L0_attn_w_out, L0_attn_b_out, L0_ffn_router_weight, L0_ffn_w1,
                 L0_ffn_w2):
    f32 = np.float32
    cc = np.zeros((128, 8), f32)
    cc[:, C_ONE] = 1.0
    cc[:, C_NEG1] = -1.0
    cc[:, C_HALF] = 0.5
    cc[:, C_NEGHALF] = -0.5
    cc[:, C_PIH] = PI / 2
    cc[:, C_EPS] = 1e-5
    cc[:, C_NEG2PI] = -2 * PI
    cc[:, C_I2PI] = 1.0 / (2 * PI)
    shared = dict(
        w_in=np.ascontiguousarray(L0_attn_w_in, f32),
        b_in=np.ascontiguousarray(np.asarray(L0_attn_b_in, f32).reshape(16, 128).T),
        w_gx=np.ascontiguousarray(L0_attn_w_gx, f32),
        b_gx=np.ascontiguousarray(np.asarray(L0_attn_b_gx, f32).reshape(8, 128).T),
        w_ga=np.ascontiguousarray(L0_attn_w_ga, f32),
        b_ga=np.ascontiguousarray(np.asarray(L0_attn_b_ga, f32).reshape(8, 128).T),
        arp=np.ascontiguousarray(np.asarray(L0_attn_a_real_param, f32).reshape(8, 128).T),
        aip=np.ascontiguousarray(np.asarray(L0_attn_a_imag_param, f32).reshape(8, 128).T),
        w_out=np.ascontiguousarray(L0_attn_w_out, f32),
        bout_pl=np.ascontiguousarray(
            np.broadcast_to(np.asarray(L0_attn_b_out, f32), (128, D))),
        rw=np.ascontiguousarray(L0_ffn_router_weight, f32),
        w1=np.ascontiguousarray(L0_ffn_w1, f32),
        w2=np.ascontiguousarray(L0_ffn_w2, f32),
        cc=cc,
    )
    x = np.ascontiguousarray(np.asarray(L0_attn_x, f32))
    in_maps = []
    for i in range(8):
        m = dict(shared)
        m["x"] = np.ascontiguousarray(x[2 * i:2 * i + 2])
        in_maps.append(m)
    return in_maps


def kernel(**inputs):
    in_maps = make_in_maps(**inputs)
    nc = build_nc()
    res = run_bass_kernel_spmd(nc, in_maps, core_ids=list(range(8)))
    return np.concatenate([np.asarray(r["y"], np.float32) for r in res.results], axis=0)


# version 12 (final)
# speedup vs baseline: 1.1754x; 1.1754x over previous
import contextlib
import numpy as np
import ml_dtypes
import concourse.bass as bass
import concourse.mybir as mybir
import concourse.bass_utils as _bass_utils
from concourse.bass_utils import run_bass_kernel_spmd

if not getattr(_bass_utils, "_ldw_opt_patched", False):
    _orig_run_command = _bass_utils.run_command

    def _run_command_ldw(cmd, *a, **kw):
        if isinstance(cmd, list):
            cmd = ["--enable-ldw-opt=true" if c == "--enable-ldw-opt=false" else c
                   for c in cmd]
        return _orig_run_command(cmd, *a, **kw)

    _bass_utils.run_command = _run_command_ldw
    _bass_utils._ldw_opt_patched = True

F32 = mybir.dt.float32
BF16 = mybir.dt.bfloat16
I32 = mybir.dt.int32
AF = mybir.ActivationFunctionType
ALU = mybir.AluOpType
AX = mybir.AxisListType

PI = float(np.pi)

SEQ = 2
T = 4096
D = 1024
DH = 2048
NB = 8
E = 8
FF = 1408
L1 = 256
NT1 = L1 // 128
NC1 = SEQ * T // L1
L2 = 512
NT2 = L2 // 128
NC2 = SEQ * T // L2

C_ONE, C_NEG1, C_HALF, C_NEGHALF, C_PIH, C_EPS, C_NEG2PI, C_I2PI = range(8)

ENGS = ("sync", "gpsimd", "tensor", "vector", "scalar")


class Ctr:
    def __init__(self, nc, name):
        self.sem = nc.alloc_semaphore(name)
        self.n = 0


class Sched:
    def __init__(self):
        self.q = {e: [] for e in ENGS}

    def on(self, eng, fn):
        self.q[eng].append(fn)

    def wait(self, eng, ctr, val=None):
        v = ctr.n if val is None else val
        if v <= 0:
            return
        sem = ctr.sem
        self.on(eng, lambda e, s=sem, vv=v: e.wait_ge(s, vv))

    def op(self, eng, fn, inc=None, k=1):
        if inc is not None:
            inc.n += k
            sem = inc.sem
            self.on(eng, lambda e, f=fn, s=sem, kk=k: f(e).then_inc(s, kk))
        else:
            self.on(eng, fn)

    def dma(self, eng, out, in_, inc):
        inc.n += 16
        sem = inc.sem
        self.on(eng, lambda e, o=out, i=in_, s=sem: e.dma_start(out=o, in_=i).then_inc(s, 16))

    def dmat(self, eng, out, in_, inc):
        inc.n += 16
        sem = inc.sem
        self.on(eng, lambda e, o=out, i=in_, s=sem: e.dma_start_transpose(o, i).then_inc(s, 16))

    def emit(self, nc, name):
        with nc.Block(name) as block:
            @block.sync
            def _(e):
                for fn in self.q["sync"]:
                    fn(e)

            @block.gpsimd
            def _(e):
                for fn in self.q["gpsimd"]:
                    fn(e)

            @block.tensor
            def _(e):
                for fn in self.q["tensor"]:
                    fn(e)

            @block.vector
            def _(e):
                for fn in self.q["vector"]:
                    fn(e)

            @block.scalar
            def _(e):
                for fn in self.q["scalar"]:
                    fn(e)


def build_nc(debug=False, no_conv=False):
    nc = bass.Bass("TRN2", target_bir_lowering=False)

    x_h = nc.dram_tensor("x", [SEQ, T, D], F32, kind="ExternalInput").ap()
    w_in_h = nc.dram_tensor("w_in", [D, DH], F32, kind="ExternalInput").ap()
    b_in_h = nc.dram_tensor("b_in", [128, 16], F32, kind="ExternalInput").ap()
    w_gx_h = nc.dram_tensor("w_gx", [NB, 256, 128], F32, kind="ExternalInput").ap()
    b_gx_h = nc.dram_tensor("b_gx", [128, 8], F32, kind="ExternalInput").ap()
    w_ga_h = nc.dram_tensor("w_ga", [NB, 256, 128], F32, kind="ExternalInput").ap()
    b_ga_h = nc.dram_tensor("b_ga", [128, 8], F32, kind="ExternalInput").ap()
    arp_h = nc.dram_tensor("arp", [128, 8], F32, kind="ExternalInput").ap()
    aip_h = nc.dram_tensor("aip", [128, 8], F32, kind="ExternalInput").ap()
    w_out_h = nc.dram_tensor("w_out", [DH, D], F32, kind="ExternalInput").ap()
    bout_pl_h = nc.dram_tensor("bout_pl", [128, D], F32, kind="ExternalInput").ap()
    rw_h = nc.dram_tensor("rw", [D, E], F32, kind="ExternalInput").ap()
    w1_h = nc.dram_tensor("w1", [E, D, 2 * FF], F32, kind="ExternalInput").ap()
    w2_h = nc.dram_tensor("w2", [E, FF, D], F32, kind="ExternalInput").ap()
    cc_h = nc.dram_tensor("cc", [128, 8], F32, kind="ExternalInput").ap()

    y_h = nc.dram_tensor("y", [SEQ, T, D], F32, kind="ExternalOutput").ap()

    _scr = "ExternalOutput" if debug else "Internal"
    w1bf_h = nc.dram_tensor("w1bf", [E, 8, 128, 2 * FF], BF16, kind=_scr).ap()
    w2bf_h = nc.dram_tensor("w2bf", [E, 11, 128, D], BF16, kind=_scr).ap()
    xnscr_h = nc.dram_tensor("xnscr", [NC1, 128, 8, L1], BF16, kind=_scr).ap()
    pscr_h = nc.dram_tensor("pscr", [NC1, 128, NT1, D], F32, kind=_scr).ap()
    if debug:
        dbg_lnc = nc.dram_tensor("dbg_lnc", [128, NT1, 8], F32, kind="ExternalOutput").ap()
        dbg_xntm = nc.dram_tensor("dbg_xntm", [128, NT1, D], BF16, kind="ExternalOutput").ap()
        dbg_xtm = nc.dram_tensor("dbg_xtm", [128, NT1, D], F32, kind="ExternalOutput").ap()
        dbg_sq = nc.dram_tensor("dbg_sq", [128, 2048], F32, kind="ExternalOutput").ap()
        dbg_xh = nc.dram_tensor("dbg_xh", [128, 16, L1], F32, kind="ExternalOutput").ap()
        dbg_ga = nc.dram_tensor("dbg_ga", [128, 8, L1], F32, kind="ExternalOutput").ap()
        dbg_gx = nc.dram_tensor("dbg_gx", [128, 8, L1], F32, kind="ExternalOutput").ap()
        dbg_h = nc.dram_tensor("dbg_h", [128, 16, L1], BF16, kind="ExternalOutput").ap()
        dbg_msp = nc.dram_tensor("dbg_msp", [128, 8], F32, kind="ExternalOutput").ap()
        dbg_thc = nc.dram_tensor("dbg_thc", [128, 8, L1], F32, kind="ExternalOutput").ap()
        dbg_phi = nc.dram_tensor("dbg_phi", [128, 8, L1], F32, kind="ExternalOutput").ap()
        dbg_rho = nc.dram_tensor("dbg_rho", [128, 8, L1], F32, kind="ExternalOutput").ap()
        dbg_nrm = nc.dram_tensor("dbg_nrm", [128, 8, L1], F32, kind="ExternalOutput").ap()
        dbg_sp = nc.dram_tensor("dbg_sp", [128, 8, L1], F32, kind="ExternalOutput").ap()
        dbg_cp = nc.dram_tensor("dbg_cp", [128, 8, L1], F32, kind="ExternalOutput").ap()
        dbg_s2 = nc.dram_tensor("dbg_s2", [NT1, 128, 1], F32, kind="ExternalOutput").ap()

    cc = nc.alloc_sbuf_tensor("cc_sb", [128, 8], F32)
    bout_pl = nc.alloc_sbuf_tensor("bout_sb", [128, D], F32)
    rw_sb = nc.alloc_sbuf_tensor("rw_sb", [128, 8, E], BF16)
    bin_c = nc.alloc_sbuf_tensor("bin_c", [128, 16], F32)
    bgx_c = nc.alloc_sbuf_tensor("bgx_c", [128, 8], F32)
    bga_c = nc.alloc_sbuf_tensor("bga_c", [128, 8], F32)
    arp_c = nc.alloc_sbuf_tensor("arp_c", [128, 8], F32)
    aip_c = nc.alloc_sbuf_tensor("aip_c", [128, 8], F32)
    msp_c = nc.alloc_sbuf_tensor("msp_c", [128, 8], F32)
    tc0 = nc.alloc_sbuf_tensor("tc0", [128, 8], F32)
    tc1 = nc.alloc_sbuf_tensor("tc1", [128, 8], F32)
    tc2 = nc.alloc_sbuf_tensor("tc2", [128, 8], F32)
    tc3 = nc.alloc_sbuf_tensor("tc3", [128, 8], F32)
    bar = nc.alloc_sbuf_tensor("bar", [128, 8], F32)
    ones_pl = nc.alloc_sbuf_tensor("ones_pl", [128, L1], F32)

    s_ld = Ctr(nc, "s_ld")
    s_gw = Ctr(nc, "s_gw")
    s_pe = Ctr(nc, "s_pe")
    s_dv = Ctr(nc, "s_dv")
    s_ac = Ctr(nc, "s_ac")
    s_st = Ctr(nc, "s_st")
    s_ph = Ctr(nc, "s_ph")
    s_p2l = Ctr(nc, "s_p2l")
    s_gp = Ctr(nc, "s_gp")

    def ccol(i):
        return cc[:, i:i + 1]

    def gflush(s, inc):
        s.op("gpsimd", lambda g: g.tensor_copy(bar[:, 6:7], cc[:, 0:1]), inc=inc)

    def vflush(s, inc):
        s.op("vector", lambda v: v.tensor_copy(bar[:, 2:3], cc[:, 0:1]), inc=inc)

    def aflush(s, inc):
        s.op("scalar", lambda sc: sc.copy(bar[:, 3:4], cc[:, 0:1]), inc=inc)

    s = Sched()
    stack = contextlib.ExitStack()
    w_in_sb = stack.enter_context(nc.sbuf_tensor("w_in_sb", [128, 8, DH], BF16))
    wgx_sb = stack.enter_context(nc.sbuf_tensor("wgx_sb", [128, NB, 2, 128], BF16))
    wga_sb = stack.enter_context(nc.sbuf_tensor("wga_sb", [128, NB, 2, 128], BF16))
    w_out_sb = stack.enter_context(nc.sbuf_tensor("w_out_sb", [128, 16, D], BF16))
    x_tm2 = [stack.enter_context(nc.sbuf_tensor(f"x_tm{i}", [128, NT1, D], F32))
             for i in range(2)]
    xn_tm = stack.enter_context(nc.sbuf_tensor("xn_tm", [128, NT1, D], BF16))
    xn_fm = stack.enter_context(nc.sbuf_tensor("xn_fm", [128, 8, L1], BF16))
    xh_f = stack.enter_context(nc.sbuf_tensor("xh_f", [128, 16, L1], F32))
    xh_b = stack.enter_context(nc.sbuf_tensor("xh_b", [128, 16, L1], BF16))
    GA = stack.enter_context(nc.sbuf_tensor("GA", [128, 8, L1], F32))
    GX = stack.enter_context(nc.sbuf_tensor("GX", [128, 8, L1], F32))
    T0 = stack.enter_context(nc.sbuf_tensor("T0", [128, 8, L1], F32))
    T1 = stack.enter_context(nc.sbuf_tensor("T1", [128, 8, L1], F32))
    T2 = stack.enter_context(nc.sbuf_tensor("T2", [128, 8, L1], F32))
    T3 = stack.enter_context(nc.sbuf_tensor("T3", [128, 8, L1], F32))
    T4 = stack.enter_context(nc.sbuf_tensor("T4", [128, 8, L1], F32))
    T5 = stack.enter_context(nc.sbuf_tensor("T5", [128, 8, L1], F32))
    h_bf = stack.enter_context(nc.sbuf_tensor("h_bf", [128, 16, L1], BF16))
    partial = stack.enter_context(nc.sbuf_tensor("partial", [128, NT1, D], F32))
    carry_r = stack.enter_context(nc.sbuf_tensor("carry_r", [128, 8], F32))
    carry_i = stack.enter_context(nc.sbuf_tensor("carry_i", [128, 8], F32))
    lnc = stack.enter_context(nc.sbuf_tensor("lnc", [128, NT1, 8], F32))
    s2c = [stack.enter_context(nc.sbuf_tensor(f"s2c{i}", [128, 1], F32))
           for i in range(NT1)]
    p_m0 = stack.enter_context(nc.psum_tensor("p_m0", [128, 512], F32))
    p_m1 = stack.enter_context(nc.psum_tensor("p_m1", [128, 512], F32))
    p_w0 = stack.enter_context(nc.psum_tensor("p_w0", [128, D], F32))
    p_w1 = stack.enter_context(nc.psum_tensor("p_w1", [128, D], F32))
    p_m = [p_m0, p_m1]
    p_w = [p_w0, p_w1]

    KIap = T5.ap().bitcast(I32)

    s.dma("sync", cc[:], cc_h, s_ld)
    s.dma("sync", bout_pl[:], bout_pl_h, s_ld)
    s.dma("sync", bin_c[:], b_in_h, s_ld)
    s.dma("sync", bgx_c[:], b_gx_h, s_ld)
    s.dma("sync", bga_c[:], b_ga_h, s_ld)
    s.dma("sync", arp_c[:], arp_h, s_ld)
    s.dma("sync", aip_c[:], aip_h, s_ld)
    LD_CONSTS = s_ld.n

    s.dma("gpsimd", w_in_sb[:], w_in_h.rearrange("(k p) h -> p k h", p=128), s_gw)
    s.dma("gpsimd", wgx_sb[:], w_gx_h.rearrange("b (kt p) o -> p b kt o", p=128), s_gw)
    s.dma("gpsimd", wga_sb[:], w_ga_h.rearrange("b (kt p) o -> p b kt o", p=128), s_gw)
    s.dma("gpsimd", w_out_sb[:], w_out_h.rearrange("(k p) h -> p k h", p=128), s_gw)
    s.dma("gpsimd", rw_sb[:], rw_h.rearrange("(k p) e -> p k e", p=128), s_gw)
    GW_ATTN = s_gw.n
    if not no_conv:
        for e in range(E):
            s.dma("gpsimd", w1bf_h[e], w1_h[e].rearrange("(kt p) f -> kt p f", p=128), s_gw)
            s.dma("gpsimd", w2bf_h[e], w2_h[e].rearrange("(ft p) h -> ft p h", p=128), s_gw)
    GW_CONV = s_gw.n

    s.wait("vector", s_ld, LD_CONSTS)
    s.op("vector", lambda v: v.memset(ones_pl[:], 1.0))
    s.op("vector", lambda v: v.memset(carry_r[:], 0.0))
    s.op("vector", lambda v: v.memset(carry_i[:], 0.0), inc=s_dv)
    V_INIT = s_dv.n
    s.wait("scalar", s_ld, LD_CONSTS)
    s.op("scalar", lambda sc: sc.activation(tc0[:], arp_c[:], AF.Abs))
    s.op("scalar", lambda sc: sc.activation(tc3[:], arp_c[:], AF.Relu))
    s.op("scalar", lambda sc: sc.activation(tc1[:], tc0[:], AF.Exp, scale=ccol(C_NEG1)))
    s.op("scalar", lambda sc: sc.copy(bar[:, 4:5], cc[:, 0:1]))
    s.op("scalar", lambda sc: sc.activation(tc0[:], tc1[:], AF.Ln, bias=ccol(C_ONE)))
    aflush(s, s_ac)
    A_SP = s_ac.n
    s.wait("vector", s_ac, A_SP)
    s.op("vector", lambda v: v.tensor_add(msp_c[:], tc0[:], tc3[:]))
    vflush(s, s_dv)
    s.op("vector", lambda v: v.tensor_single_scalar(msp_c[:], msp_c[:], -8.0, ALU.mult))
    vflush(s, s_dv)
    V_MSP = s_dv.n

    pm_free = [("", 0), ("", 0)]
    pw_free = [0, 0]
    V_end = 0
    V_scans_prev = 0
    G_carry = 0
    G_rot = 0
    Vend_hist = [0] * (NC1 + 2)
    L_t_prev = 0
    M_xn_prev = 0
    P_bdl_prev = 0
    P_wo_prev = 0
    S_p_prev = 0
    mm_rot = 0
    wo_rot = 0

    x_r = x_h.rearrange("s (c tt p) ch -> (s c) p tt ch", tt=NT1, p=128)

    xr = xh_f.ap().rearrange("p (a ct) t -> p a ct t", a=2)[:, 0]
    xi = xh_f.ap().rearrange("p (a ct) t -> p a ct t", a=2)[:, 1]
    hr = h_bf.ap().rearrange("p (a ct) t -> p a ct t", a=2)[:, 0]
    hi = h_bf.ap().rearrange("p (a ct) t -> p a ct t", a=2)[:, 1]
    mspb = msp_c.ap().broadcast_to([128, 8, L1])
    aipb = aip_c.ap().broadcast_to([128, 8, L1])
    sqflat = T0.ap().rearrange("p a b -> p (a b)")
    sqv = [sqflat[:, tt * D:(tt + 1) * D] for tt in range(NT1)]

    for c in range(NC1):
        ci = c % (NC1 // SEQ)
        x_tm = x_tm2[c % 2]
        if c >= 2:
            s.wait("sync", s_dv, Vend_hist[c - 2])
        s.dma("sync", x_tm[:], x_r[c], s_ld)
        L_x = s_ld.n

        if ci == 0 and c > 0:
            s.op("vector", lambda v: v.memset(carry_r[:], 0.0))
            s.op("vector", lambda v: v.memset(carry_i[:], 0.0), inc=s_dv)

        s.wait("vector", s_ld, L_x)
        s.wait("vector", s_ld, L_t_prev)
        for tt in range(NT1):
            s.op("vector", lambda v, tt=tt, xtm=x_tm: v.reduce_sum(
                lnc[:, tt, 0:1], xtm[:, tt, :], AX.X), inc=s_dv)
        s.wait("scalar", s_ld, L_x)
        s.wait("scalar", s_gp, G_carry)
        for tt in range(NT1):
            s.op("scalar", lambda sc, tt=tt, xtm=x_tm: sc.activation(
                sqv[tt], xtm[:, tt, :], AF.Square), inc=s_ac)
            A_sq = s_ac.n
            s.wait("vector", s_ac, A_sq)
            s.op("vector", lambda v, tt=tt: v.reduce_sum(s2c[tt][:], sqv[tt], AX.X))
        for tt in range(NT1):
            s.op("vector", lambda v, tt=tt: v.tensor_single_scalar(
                lnc[:, tt, 2:3], lnc[:, tt, 0:1], 1.0 / D, ALU.mult))
        for tt in range(NT1):
            s.op("vector", lambda v, tt=tt: v.tensor_single_scalar(
                lnc[:, tt, 5:6], s2c[tt][:], 1.0 / D, ALU.mult))
        for tt in range(NT1):
            s.op("vector", lambda v, tt=tt: v.tensor_mul(
                lnc[:, tt, 3:4], lnc[:, tt, 2:3], lnc[:, tt, 2:3]))
        for tt in range(NT1):
            s.op("vector", lambda v, tt=tt: v.tensor_sub(
                lnc[:, tt, 1:2], lnc[:, tt, 5:6], lnc[:, tt, 3:4]))
        vflush(s, s_dv)
        V_var = s_dv.n
        s.wait("scalar", s_dv, V_var)
        for tt in range(NT1):
            s.op("scalar", lambda sc, tt=tt: sc.activation(
                lnc[:, tt, 4:5], lnc[:, tt, 1:2], AF.Ln, bias=ccol(C_EPS)))
        for tt in range(NT1):
            s.op("scalar", lambda sc, tt=tt: sc.activation(
                lnc[:, tt, 0:1], lnc[:, tt, 4:5], AF.Exp, scale=ccol(C_NEGHALF)))
        aflush(s, s_ac)
        A_rstd = s_ac.n
        s.wait("vector", s_ac, A_rstd)
        for tt in range(NT1):
            s.op("vector", lambda v, tt=tt, xtm=x_tm: v.tensor_scalar(
                xn_tm[:, tt, :], xtm[:, tt, :], lnc[:, tt, 2:3], lnc[:, tt, 0:1],
                ALU.subtract, ALU.mult), inc=s_dv)
        V_ln = s_dv.n

        s.wait("sync", s_dv, V_ln)
        if debug and c == 0:
            s.dma("sync", dbg_lnc, lnc[:], s_ld)
            s.dma("sync", dbg_xntm, xn_tm[:], s_ld)
            s.dma("sync", dbg_xtm, x_tm[:], s_ld)
            s.dma("sync", dbg_sq, sqflat, s_ld)
            for tt in range(NT1):
                s.dma("sync", dbg_s2[tt], s2c[tt][:], s_ld)
            s.wait("vector", s_ld)
            s.wait("scalar", s_ld)
        s.wait("sync", s_pe, M_xn_prev)
        for tt in range(NT1):
            s.dmat("sync", xn_fm[:, :, tt * 128:(tt + 1) * 128], xn_tm[:, tt, :], s_ld)
        L_t = s_ld.n
        s.wait("sync", s_ld, L_t)
        s.dma("sync", xnscr_h[c], xn_fm[:], s_ld)
        L_t_prev = s_ld.n

        if c == 0:
            s.wait("tensor", s_gw, GW_ATTN)
        s.wait("tensor", s_ld, L_t)
        for ht in range(16):
            bk = mm_rot % 2
            mm_rot += 1
            nm, v = pm_free[bk]
            if nm == "ac":
                s.wait("tensor", s_ac, v)
            for kt in range(8):
                s.op("tensor", lambda t, ht=ht, kt=kt, bk=bk: t.matmul(
                    p_m[bk][:, 0:L1], w_in_sb[:, kt, ht * 128:(ht + 1) * 128],
                    xn_fm[:, kt, :], start=(kt == 0), stop=(kt == 7)),
                    inc=(s_pe if kt == 7 else None))
            P_ht = s_pe.n
            s.wait("vector", s_pe, P_ht)
            if ht == 0:
                s.wait("vector", s_gp, G_rot)
            s.op("vector", lambda v, ht=ht, bk=bk: v.tensor_scalar(
                xh_f[:, ht, :], p_m[bk][:, 0:L1], bin_c[:, ht:ht + 1], None,
                ALU.add), inc=s_dv)
            V_xh = s_dv.n
            if ht == 0:
                s.wait("scalar", s_pe, P_bdl_prev)
                s.wait("scalar", s_gp, G_carry)
            s.wait("scalar", s_dv, V_xh)
            s.op("scalar", lambda sc, ht=ht, bk=bk: sc.activation(
                xh_b[:, ht, :], p_m[bk][:, 0:L1], AF.Identity,
                bias=bin_c[:, ht:ht + 1]), inc=s_ac)
            pm_free[bk] = ("ac", s_ac.n)

        s.wait("scalar", s_dv, V_scans_prev)
        s.wait("scalar", s_gp, G_carry)
        for wsb, bcol, gpl in ((wgx_sb, bgx_c, GX), (wga_sb, bga_c, GA)):
            for b in range(NB):
                bk = mm_rot % 2
                mm_rot += 1
                nm, v = pm_free[bk]
                if nm == "ac":
                    s.wait("tensor", s_ac, v)
                for kt in range(2):
                    s.op("tensor", lambda t, wsb=wsb, b=b, kt=kt, bk=bk: t.matmul(
                        p_m[bk][:, 0:L1], wsb[:, b, kt, :],
                        xh_b[:, b * 2 + kt, :], start=(kt == 0), stop=(kt == 1)),
                        inc=(s_pe if kt == 1 else None))
                P_g = s_pe.n
                s.wait("scalar", s_pe, P_g)
                s.op("scalar", lambda sc, gpl=gpl, b=b, bk=bk, bcol=bcol: sc.activation(
                    gpl[:, b, :], p_m[bk][:, 0:L1], AF.Sigmoid,
                    bias=bcol[:, b:b + 1]), inc=s_ac)
                pm_free[bk] = ("ac", s_ac.n)
        P_bdl_prev = s_pe.n
        A_gates = s_ac.n
        M_xn_prev = s_pe.n

        if debug and c == 0:
            s.wait("sync", s_ac, A_gates)
            s.dma("sync", dbg_xh, xh_f[:], s_ld)
            s.dma("sync", dbg_ga, GA[:], s_ld)
            s.dma("sync", dbg_gx, GX[:], s_ld)
            s.dma("sync", dbg_msp, msp_c[:], s_ld)
            s.wait("vector", s_ld)
            s.wait("scalar", s_ld)
            s.wait("tensor", s_ld)
        if c == 0:
            s.wait("vector", s_dv, max(V_MSP, V_INIT))
        s.wait("vector", s_ac, A_gates)
        s.op("vector", lambda v: v.tensor_tensor(T0[:], GA[:], mspb, ALU.mult))
        s.op("vector", lambda v: v.tensor_tensor(T1[:], GA[:], aipb, ALU.mult),
             inc=s_dv)
        V_l = s_dv.n
        s.wait("vector", s_gp, G_carry)
        for ct in range(8):
            s.op("vector", lambda v, ct=ct: v.tensor_tensor_scan(
                T2[:, ct, :], ones_pl[:], T1[:, ct, :], 0.0, ALU.mult, ALU.add))
        s.op("vector", lambda v: v.tensor_single_scalar(
            T1[:], T2[:], 1.0 / (2 * PI), ALU.mult))
        s.op("vector", lambda v: v.tensor_copy(KIap[:], T1[:]))
        s.op("vector", lambda v: v.tensor_copy(T1[:], KIap[:]))
        s.op("vector", lambda v: v.scalar_tensor_tensor(
            T1[:], T1[:], -2 * PI, T2[:], ALU.mult, ALU.add), inc=s_dv)
        V_phi = s_dv.n
        if debug and c == 0:
            s.wait("sync", s_dv, V_phi)
            s.dma("sync", dbg_thc, T2[:], s_ld)
            s.dma("sync", dbg_phi, T1[:], s_ld)
            s.wait("vector", s_ld)
            s.wait("scalar", s_ld)
        s.wait("scalar", s_dv, V_phi)
        s.op("scalar", lambda sc: sc.activation(T2[:], T0[:], AF.Exp))
        s.op("scalar", lambda sc: sc.activation(T0[:], T2[:], AF.Square), inc=s_ac)
        A_q = s_ac.n
        s.wait("vector", s_ac, A_q)
        s.op("vector", lambda v: v.tensor_scalar(
            T0[:], T0[:], -1.0, 1.0, ALU.mult, ALU.add))
        s.op("vector", lambda v: v.tensor_single_scalar(
            T0[:], T0[:], 1e-6, ALU.max), inc=s_dv)
        V_r2 = s_dv.n
        s.wait("scalar", s_dv, V_r2)
        s.op("scalar", lambda sc: sc.activation(T3[:], T0[:], AF.Ln))
        s.op("scalar", lambda sc: sc.activation(T0[:], T3[:], AF.Exp,
                                                scale=ccol(C_HALF)), inc=s_ac)
        s.wait("scalar", s_dv, V_phi)
        s.op("scalar", lambda sc: sc.activation(T3[:], T1[:], AF.Sin))
        s.op("scalar", lambda sc: sc.activation(T5[:], T1[:], AF.Abs))
        s.op("scalar", lambda sc: sc.activation(
            T4[:], T5[:], AF.Sin, scale=ccol(C_NEG1), bias=ccol(C_PIH)), inc=s_ac)
        A_trig = s_ac.n
        if debug and c == 0:
            s.wait("sync", s_ac, A_trig)
            s.dma("sync", dbg_rho, T2[:], s_ld)
            s.dma("sync", dbg_nrm, T0[:], s_ld)
            s.dma("sync", dbg_sp, T3[:], s_ld)
            s.dma("sync", dbg_cp, T4[:], s_ld)
            s.wait("vector", s_ld)
            s.wait("scalar", s_ld)
        s.wait("vector", s_ac, A_trig)
        s.op("vector", lambda v: v.tensor_mul(T1[:], GX[:], T0[:]))
        s.op("vector", lambda v: v.tensor_mul(GX[:], T4[:], T1[:]))
        s.op("vector", lambda v: v.tensor_mul(T0[:], T3[:], T1[:]))
        s.op("vector", lambda v: v.tensor_mul(T1[:], GX[:], xr))
        s.op("vector", lambda v: v.tensor_mul(T5[:], T0[:], xi))
        s.op("vector", lambda v: v.tensor_add(T1[:], T1[:], T5[:]))
        s.op("vector", lambda v: v.tensor_mul(GA[:], GX[:], xi))
        s.op("vector", lambda v: v.tensor_mul(T5[:], T0[:], xr))
        s.op("vector", lambda v: v.tensor_sub(GA[:], GA[:], T5[:]), inc=s_gp)
        G_rot = s_gp.n
        for ct in range(8):
            s.op("vector", lambda v, ct=ct: v.tensor_tensor_scan(
                GX[:, ct, :], T2[:, ct, :], T1[:, ct, :], carry_r[:, ct:ct + 1],
                ALU.mult, ALU.add))
        for ct in range(8):
            s.op("vector", lambda v, ct=ct: v.tensor_tensor_scan(
                T0[:, ct, :], T2[:, ct, :], GA[:, ct, :], carry_i[:, ct:ct + 1],
                ALU.mult, ALU.add))
        vflush(s, s_dv)
        V_scans = s_dv.n
        V_scans_prev = V_scans
        s.wait("vector", s_pe, P_wo_prev)
        s.op("vector", lambda v: v.tensor_mul(T1[:], T4[:], GX[:]))
        s.op("vector", lambda v: v.tensor_mul(T2[:], T3[:], T0[:]))
        s.op("vector", lambda v: v.tensor_sub(hr, T1[:], T2[:]))
        s.op("vector", lambda v: v.tensor_mul(T1[:], T3[:], GX[:]))
        s.op("vector", lambda v: v.tensor_mul(T2[:], T4[:], T0[:]))
        s.op("vector", lambda v: v.tensor_add(hi, T1[:], T2[:]))
        LC = L1 - 1
        s.op("vector", lambda v: v.tensor_mul(tc0[:], T4[:, :, LC], GX[:, :, LC]))
        s.op("vector", lambda v: v.tensor_mul(tc1[:], T3[:, :, LC], T0[:, :, LC]))
        s.op("vector", lambda v: v.tensor_mul(tc2[:], T3[:, :, LC], GX[:, :, LC]))
        s.op("vector", lambda v: v.tensor_mul(tc3[:], T4[:, :, LC], T0[:, :, LC]))
        s.op("vector", lambda v: v.tensor_sub(carry_r[:], tc0[:], tc1[:]))
        s.op("vector", lambda v: v.tensor_add(carry_i[:], tc2[:], tc3[:]))
        vflush(s, s_gp)
        G_carry = s_gp.n
        V_h = s_gp.n

        if debug and c == 0:
            s.wait("sync", s_gp, V_h)
            s.dma("sync", dbg_h, h_bf[:], s_ld)
            s.wait("vector", s_ld)
            s.wait("scalar", s_ld)
            s.wait("tensor", s_ld)
        s.wait("tensor", s_gp, V_h)
        for tt in range(NT1):
            bk = wo_rot % 2
            wo_rot += 1
            if pw_free[bk]:
                s.wait("tensor", s_dv, pw_free[bk])
            for kt in range(16):
                for nh in range(2):
                    s.op("tensor", lambda t, tt=tt, nh=nh, kt=kt, bk=bk: t.matmul(
                        p_w[bk][:, nh * 512:(nh + 1) * 512],
                        h_bf[:, kt, tt * 128:(tt + 1) * 128],
                        w_out_sb[:, kt, nh * 512:(nh + 1) * 512],
                        start=(kt == 0), stop=(kt == 15)),
                        inc=(s_pe if (nh == 1 and kt == 15) else None))
            P_wo = s_pe.n
            s.wait("vector", s_pe, P_wo)
            if tt == 0 and S_p_prev:
                s.wait("vector", s_st, S_p_prev)
            for nh in range(2):
                sl = slice(nh * 512, (nh + 1) * 512)
                s.op("vector", lambda v, tt=tt, bk=bk, sl=sl, xtm=x_tm: v.scalar_tensor_tensor(
                    partial[:, tt, sl], p_w[bk][:, sl], 1.0, xtm[:, tt, sl],
                    ALU.mult, ALU.add))
                s.op("vector", lambda v, tt=tt, sl=sl: v.tensor_add(
                    partial[:, tt, sl], partial[:, tt, sl], bout_pl[:, sl]),
                    inc=s_dv)
            pw_free[bk] = s_dv.n
        P_wo_prev = s_pe.n
        V_end = s_dv.n
        Vend_hist[c] = V_end

        s.wait("sync", s_dv, V_end)
        s.dma("sync", pscr_h[c], partial[:], s_st)
        S_p_prev = s_st.n

    s.wait("sync", s_st, s_st.n)
    s.wait("sync", s_ld, s_ld.n)
    s.op("sync", lambda e: e.nop(), inc=s_ph)
    s.op("gpsimd", lambda e: e.nop(), inc=s_ph)
    s.op("tensor", lambda e: e.nop(), inc=s_ph)
    s.op("vector", lambda v: v.memset(bar[:, 0:1], 1.0), inc=s_ph)
    s.op("scalar", lambda sc: sc.memzero(bar[:, 1:2]), inc=s_ph)
    PH_BAR = s_ph.n

    s.emit(nc, "ph1")
    stack.close()

    s = Sched()
    stack = contextlib.ExitStack()
    xn2 = [stack.enter_context(nc.sbuf_tensor(f"xn2_{i}", [128, 8, L2], BF16))
           for i in range(2)]
    part2 = stack.enter_context(nc.sbuf_tensor("part2", [128, NT2, D], F32))
    gbuf = stack.enter_context(nc.sbuf_tensor("gbuf", [128, 11, L2], BF16))
    silu_sb = [stack.enter_context(nc.sbuf_tensor(f"silu_{i}", [128, L2], F32))
               for i in range(2)]
    w1buf = [stack.enter_context(nc.sbuf_tensor(f"w1buf_{i}", [128, 8, 2 * FF], BF16))
             for i in range(2)]
    w2buf = [stack.enter_context(nc.sbuf_tensor(f"w2buf_{i}", [128, 11, D], BF16))
             for i in range(2)]
    lp = stack.enter_context(nc.sbuf_tensor("lp", [128, NT2, E], F32))
    pr4 = stack.enter_context(nc.sbuf_tensor("pr4", [128, NT2, E], F32))
    sm0 = stack.enter_context(nc.sbuf_tensor("sm0", [128, NT2], F32))
    sm1 = stack.enter_context(nc.sbuf_tensor("sm1", [128, NT2], F32))
    p_y00 = stack.enter_context(nc.psum_tensor("p_y00", [128, 512], F32))
    p_y01 = stack.enter_context(nc.psum_tensor("p_y01", [128, 512], F32))
    p_y10 = stack.enter_context(nc.psum_tensor("p_y10", [128, 512], F32))
    p_y11 = stack.enter_context(nc.psum_tensor("p_y11", [128, 512], F32))
    p_o0 = stack.enter_context(nc.psum_tensor("p_o0", [128, D], F32))
    p_o1 = stack.enter_context(nc.psum_tensor("p_o1", [128, D], F32))
    p_y = [[p_y00, p_y01], [p_y10, p_y11]]
    p_o = [p_o0, p_o1]

    for eng in ENGS:
        s.wait(eng, s_ph, PH_BAR)
    s.wait("gpsimd", s_gw, GW_CONV)

    py_free = [0, 0]
    po_free = [0, 0]
    xn2_freeby = [0, 0]
    wslot_freeby = [0, 0]
    S_part_prev = 0
    ft_rot = 0
    tt_rot = 0

    y_r = y_h.rearrange("s (k tt p) ch -> (s k) p tt ch", tt=NT2, p=128)
    prv = p_y00.ap().rearrange("p (tt e) -> p tt e", e=E)[:, 0:NT2, :]

    for c2 in range(NC2):
        buf = c2 % 2
        if xn2_freeby[buf]:
            s.wait("sync", s_pe, xn2_freeby[buf])
        for hh in range(2):
            s.dma("sync", xn2[buf][:, :, hh * L1:(hh + 1) * L1], xnscr_h[2 * c2 + hh],
                  s_ld)
        L_xn2 = s_ld.n
        if S_part_prev:
            s.wait("sync", s_st, S_part_prev)
        for hh in range(2):
            s.dma("sync", part2[:, 2 * hh:2 * hh + 2, :], pscr_h[2 * c2 + hh], s_p2l)
        P2L = s_p2l.n

        s.wait("tensor", s_ld, L_xn2)
        if py_free[0]:
            s.wait("tensor", s_dv, py_free[0])
        for tt in range(NT2):
            for kt in range(8):
                s.op("tensor", lambda t, tt=tt, kt=kt, buf=buf: t.matmul(
                    p_y00[:, tt * E:(tt + 1) * E],
                    xn2[buf][:, kt, tt * 128:(tt + 1) * 128],
                    rw_sb[:, kt, :], start=(kt == 0), stop=(kt == 7)),
                    inc=(s_pe if (tt == NT2 - 1 and kt == 7) else None))
        P_r = s_pe.n
        s.wait("vector", s_pe, P_r)
        s.op("vector", lambda v: v.tensor_reduce(
            sm0[:], prv, AX.X, ALU.max, negate=True))
        vflush(s, s_dv)
        V_nm = s_dv.n
        s.wait("scalar", s_dv, V_nm)
        for tt in range(NT2):
            s.op("scalar", lambda sc, tt=tt: sc.activation(
                pr4[:, tt, :], prv[:, tt, :], AF.Exp, bias=sm0[:, tt:tt + 1]))
        aflush(s, s_ac)
        A_exp = s_ac.n
        s.wait("vector", s_ac, A_exp)
        s.op("vector", lambda v: v.reduce_sum(sm1[:], pr4[:], AX.X), inc=s_dv)
        py_free[0] = s_dv.n
        vsp = lambda: s.op("vector", lambda v: v.tensor_copy(bar[:, 5:6], cc[:, 1:2]))
        vsp()
        s.op("vector", lambda v: v.reciprocal(sm0[:], sm1[:]))
        vsp()
        s.op("vector", lambda v: v.tensor_tensor(
            pr4[:], pr4[:], sm0.ap().broadcast_to([128, NT2, E]), ALU.mult))
        vsp()
        s.op("vector", lambda v: v.tensor_reduce(sm1[:], pr4[:], AX.X, ALU.max))
        vsp()
        s.op("vector", lambda v: v.tensor_tensor(
            lp[:], pr4[:], sm1.ap().broadcast_to([128, NT2, E]), ALU.is_lt))
        vsp()
        s.op("vector", lambda v: v.tensor_mul(lp[:], lp[:], pr4[:]))
        vsp()
        s.op("vector", lambda v: v.tensor_reduce(sm1[:], lp[:], AX.X, ALU.max))
        vsp()
        s.op("vector", lambda v: v.tensor_tensor(
            lp[:], pr4[:], sm1.ap().broadcast_to([128, NT2, E]), ALU.is_ge))
        vsp()
        s.op("vector", lambda v: v.tensor_mul(lp[:], lp[:], pr4[:]))
        vflush(s, s_dv)
        V_lp = s_dv.n

        for e in range(E):
            g = c2 * E + e
            slot = g % 2
            if wslot_freeby[slot]:
                s.wait("gpsimd", s_pe, wslot_freeby[slot])
            s.dma("gpsimd", w1buf[slot][:], w1bf_h[e].rearrange("kt p f -> p kt f"),
                  s_gw)
            s.dma("gpsimd", w2buf[slot][:], w2bf_h[e].rearrange("ft p h -> p ft h"),
                  s_gw)
            G_w = s_gw.n

            s.wait("tensor", s_gw, G_w)
            if e == 0:
                s.wait("tensor", s_dv, V_lp)
            for ft in range(11):
                par = ft_rot % 2
                ft_rot += 1
                if py_free[par]:
                    s.wait("tensor", s_dv, py_free[par])
                for half in range(2):
                    fcol = (ft if half == 0 else ft + 11) * 128
                    for kt in range(8):
                        s.op("tensor", lambda t, slot=slot, fcol=fcol, kt=kt,
                             par=par, half=half, buf=buf: t.matmul(
                            p_y[par][half][:],
                            w1buf[slot][:, kt, fcol:fcol + 128],
                            xn2[buf][:, kt, :],
                            start=(kt == 0), stop=(kt == 7)),
                            inc=(s_pe if (half == 1 and kt == 7) else None))
                P_yft = s_pe.n
                s.wait("scalar", s_pe, P_yft)
                s.op("scalar", lambda sc, par=par: sc.activation(
                    silu_sb[par][:], p_y[par][0][:], AF.Silu), inc=s_ac)
                A_sil = s_ac.n
                s.wait("vector", s_ac, A_sil)
                s.op("vector", lambda v, ft=ft, par=par: v.tensor_mul(
                    gbuf[:, ft, :], silu_sb[par][:], p_y[par][1][:]), inc=s_dv)
                py_free[par] = s_dv.n
            V_g = s_dv.n
            if e == E - 1:
                xn2_freeby[buf] = s_pe.n

            s.wait("tensor", s_dv, V_g)
            if e == 0:
                s.wait("vector", s_p2l, P2L)
            for tt in range(NT2):
                po = tt_rot % 2
                tt_rot += 1
                if po_free[po]:
                    s.wait("tensor", s_dv, po_free[po])
                for ft in range(11):
                    for half in range(2):
                        s.op("tensor", lambda t, tt=tt, ft=ft, half=half, po=po,
                             slot=slot: t.matmul(
                            p_o[po][:, half * 512:(half + 1) * 512],
                            gbuf[:, ft, tt * 128:(tt + 1) * 128],
                            w2buf[slot][:, ft, half * 512:(half + 1) * 512],
                            start=(ft == 0), stop=(ft == 10)),
                            inc=(s_pe if (ft == 10 and half == 1) else None))
                P_oms = s_pe.n
                s.wait("vector", s_pe, P_oms)
                s.op("vector", lambda v, tt=tt, po=po, e=e: v.scalar_tensor_tensor(
                    part2[:, tt, :], p_o[po][:], lp[:, tt, e:e + 1],
                    part2[:, tt, :], ALU.mult, ALU.add), inc=s_dv)
                po_free[po] = s_dv.n
            wslot_freeby[slot] = s_pe.n
        V_acc_end = s_dv.n

        s.wait("sync", s_dv, V_acc_end)
        s.dma("sync", y_r[c2], part2[:], s_st)
        S_part_prev = s_st.n

    s.wait("sync", s_st, s_st.n)

    s.emit(nc, "ph2")
    stack.close()
    return nc


def make_in_maps(L0_attn_x, L0_attn_w_in, L0_attn_b_in, L0_attn_w_gx, L0_attn_b_gx,
                 L0_attn_w_ga, L0_attn_b_ga, L0_attn_a_real_param, L0_attn_a_imag_param,
                 L0_attn_w_out, L0_attn_b_out, L0_ffn_router_weight, L0_ffn_w1,
                 L0_ffn_w2):
    f32 = np.float32
    cc = np.zeros((128, 8), f32)
    cc[:, C_ONE] = 1.0
    cc[:, C_NEG1] = -1.0
    cc[:, C_HALF] = 0.5
    cc[:, C_NEGHALF] = -0.5
    cc[:, C_PIH] = PI / 2
    cc[:, C_EPS] = 1e-5
    cc[:, C_NEG2PI] = -2 * PI
    cc[:, C_I2PI] = 1.0 / (2 * PI)
    shared = dict(
        w_in=np.ascontiguousarray(L0_attn_w_in, f32),
        b_in=np.ascontiguousarray(np.asarray(L0_attn_b_in, f32).reshape(16, 128).T),
        w_gx=np.ascontiguousarray(L0_attn_w_gx, f32),
        b_gx=np.ascontiguousarray(np.asarray(L0_attn_b_gx, f32).reshape(8, 128).T),
        w_ga=np.ascontiguousarray(L0_attn_w_ga, f32),
        b_ga=np.ascontiguousarray(np.asarray(L0_attn_b_ga, f32).reshape(8, 128).T),
        arp=np.ascontiguousarray(np.asarray(L0_attn_a_real_param, f32).reshape(8, 128).T),
        aip=np.ascontiguousarray(np.asarray(L0_attn_a_imag_param, f32).reshape(8, 128).T),
        w_out=np.ascontiguousarray(L0_attn_w_out, f32),
        bout_pl=np.ascontiguousarray(
            np.broadcast_to(np.asarray(L0_attn_b_out, f32), (128, D))),
        rw=np.ascontiguousarray(L0_ffn_router_weight, f32),
        w1=np.ascontiguousarray(L0_ffn_w1, f32),
        w2=np.ascontiguousarray(L0_ffn_w2, f32),
        cc=cc,
    )
    x = np.ascontiguousarray(np.asarray(L0_attn_x, f32))
    in_maps = []
    for i in range(8):
        m = dict(shared)
        m["x"] = np.ascontiguousarray(x[2 * i:2 * i + 2])
        in_maps.append(m)
    return in_maps


def kernel(**inputs):
    in_maps = make_in_maps(**inputs)
    nc = build_nc()
    res = run_bass_kernel_spmd(nc, in_maps, core_ids=list(range(8)))
    return np.concatenate([np.asarray(r["y"], np.float32) for r in res.results], axis=0)
